# version 12
# speedup vs baseline: 1.1202x; 1.0041x over previous
import numpy as np
from contextlib import ExitStack

import concourse.bass as bass
import concourse.mybir as mybir
from concourse.bass_utils import run_bass_kernel_spmd

F32 = mybir.dt.float32
BF16 = mybir.dt.bfloat16
OP = mybir.AluOpType
AF = mybir.ActivationFunctionType
AX = mybir.AxisListType

D = 1024
CH = 1024
NKT = 8
NDT = 8
NS = 16
DTR = 64
EPS = 1e-5
PAIRS = [[0, 1], [2, 3], [4, 5], [6, 7]]


class Sem:
    def __init__(self, st, nc, name):
        self.h = st.enter_context(nc.semaphore(name))
        self.v = 0

    def inc(self, instr, n=1):
        instr.then_inc(self.h, n)
        self.v += n
        return self.v

    def wait(self, eng, v=None):
        tv = self.v if v is None else v
        assert tv <= self.v, f"wait {tv} > emitted {self.v}"
        if tv > 0:
            eng.wait_ge(self.h, tv)


def build(T=4096, taps=False):
    TH = T // 2
    NTT = TH // 512
    NG = TH // 128
    NTC = T // 512
    NFG = TH // 512
    nc = bass.Bass("TRN2", target_bir_lowering=False, debug=False)

    dI = lambda n, s: nc.dram_tensor(n, s, F32, kind="ExternalInput").ap()
    xb = dI("xb", [T, D])
    xh = dI("xh", [TH, D])
    w_in_u = dI("w_in_u", [D, CH])
    w_in_r = dI("w_in_r", [D, CH])
    conv_w = dI("conv_w", [CH, 4])
    conv_b = dI("conv_b", [CH, 1])
    xp_w = dI("xp_w", [CH, 96])
    dt_w = dI("dt_w", [DTR, CH])
    dt_b = dI("dt_b", [CH, 1])
    a_log = dI("a_log", [CH, NS])
    d_vec = dI("d_vec", [CH, 1])
    w_out = dI("w_out", [CH, D])
    base_wt = dI("base_wt", [D, D])
    base_b = dI("base_b", [1, D])
    lora_a = dI("lora_a", [D, 8])
    lora_b2 = dI("lora_b2", [64, D])
    rout_wt = dI("rout_wt", [D, 8])
    rout_b = dI("rout_b", [1, 8])
    norm_w = dI("norm_w", [D, 1])
    ident_in = dI("ident", [128, 128])

    out_half = nc.dram_tensor("out_half", [TH, D], F32, kind="ExternalOutput").ap()

    dS = lambda n, s, dt=F32: nc.dram_tensor(n, s, dt, kind="Internal").ap()
    u_scr = dS("u_scr", [CH, T])
    res_scr = dS("res_scr", [CH, T])
    yhi_scr = dS("yhi_scr", [CH, T], BF16)
    ylo_scr = dS("ylo_scr", [CH, T], BF16)
    NBLK = TH // 512
    part_scr = dS("part_scr", [NBLK * 2 * D, 512])
    mix_scr = dS("mix_scr", [NBLK * D, 512])
    dbcp = dS("dbcp", [96, T])
    dbcs = dS("dbcs", [96, T])

    if taps:
        tap_dbc = nc.dram_tensor("tap_dbc", [96, T], F32, kind="ExternalOutput").ap()
        tap_u = nc.dram_tensor("tap_u", [CH, T], F32, kind="ExternalOutput").ap()
        tap_res = nc.dram_tensor("tap_res", [CH, T], F32, kind="ExternalOutput").ap()
        tap_delta = nc.dram_tensor("tap_delta", [CH, T], F32, kind="ExternalOutput").ap()
        tap_y = nc.dram_tensor("tap_y", [CH, T], F32, kind="ExternalOutput").ap()
        tap_mix = nc.dram_tensor("tap_mix", [D, TH], F32, kind="ExternalOutput").ap()
        tap_xn = nc.dram_tensor("tap_xn", [D, T], F32, kind="ExternalOutput").ap()

    st = ExitStack()
    with st:
        sb = lambda n, s, dt=F32: st.enter_context(nc.sbuf_tensor(n, s, dt))
        normw_sb = sb("normw", [128, NKT])
        cw_sb = sb("cw", [128, 4 * NDT])
        cb_sb = sb("cb", [128, NDT])
        dtb_sb = sb("dtb", [128, NDT])
        d_sb = sb("dsb", [128, NDT])
        a_sb = sb("asb", [128, NS * NDT])
        ident = sb("identsb", [128, 128])
        ident_bf = sb("identbf", [128, 128], BF16)
        ones1 = sb("ones1", [1, 128], BF16)
        consts = sb("consts", [128, 2])
        selbc = sb("selbc", [32, 2 * NS * 128], BF16)
        xph = sb("xph", [128, 96 * NDT], BF16)
        xpl = sb("xpl", [128, 96 * NDT], BF16)
        dtwh = sb("dtwh", [DTR, CH], BF16)
        dtwl = sb("dtwl", [DTR, CH], BF16)
        basb = sb("basb", [128, D])
        routb = sb("routb", [128, 8])
        tails = sb("tails", [128, 3 * NDT])
        state = sb("state", [128, NS * NDT])
        dth = sb("dth", [DTR, T], BF16)
        dtl = sb("dtl", [DTR, T], BF16)
        bchi = sb("bchi", [32, T], BF16)
        bclo = sb("bclo", [32, T], BF16)

        s_pe = Sem(st, nc, "pe")
        s_sc = Sem(st, nc, "sc")
        s_ve = Sem(st, nc, "ve")
        s_gf = Sem(st, nc, "gf")
        s_cc = Sem(st, nc, "cc")
        s_sA = Sem(st, nc, "sA")
        s_gB = Sem(st, nc, "gB")
        s_vS = Sem(st, nc, "vS")
        s_vP = Sem(st, nc, "vP")
        s_sBC = Sem(st, nc, "sBC")
        s_cst = Sem(st, nc, "cst")
        s_xg = [Sem(st, nc, "xga"), Sem(st, nc, "xgb")]
        s_w = [Sem(st, nc, "wa"), Sem(st, nc, "wb")]
        s_db = Sem(st, nc, "db")
        s_ur = [Sem(st, nc, "ura"), Sem(st, nc, "urb")]
        s_rr = Sem(st, nc, "rr")
        s_yr = [Sem(st, nc, "yra"), Sem(st, nc, "yrb")]
        s_wo = Sem(st, nc, "wo")
        s_bw = Sem(st, nc, "bw")
        s_xh = Sem(st, nc, "xhld")
        s_xhb = Sem(st, nc, "xhb")
        s_mx = Sem(st, nc, "mx")
        s_mxb = Sem(st, nc, "mxb")
        s_sp = Sem(st, nc, "sp")
        s_st = Sem(st, nc, "stq")
        s_stb = Sem(st, nc, "stqb")

        def gfence(*engs):
            nc.gpsimd.wait_ge(s_sp.h, s_sp.v)
            i = nc.gpsimd.engine_nop()
            s_gf.inc(i)
            for e in engs:
                s_gf.wait(e)
            return s_gf.v

        def split_hilo(src_ap, hi_ap, lo_ap, tmp_ap, wait_s=(), wait_v=()):
            s_ve.wait(nc.scalar)
            for sm, v in wait_s:
                sm.wait(nc.scalar, v)
            i1 = nc.scalar.copy(hi_ap, src_ap)
            s_sc.inc(i1)
            s_sc.wait(nc.scalar)
            i2 = nc.scalar.copy(tmp_ap, hi_ap)
            s_sc.inc(i2)
            s_sc.wait(nc.vector)
            for sm, v in wait_v:
                sm.wait(nc.vector, v)
            s_ve.wait(nc.vector)
            i3 = nc.vector.tensor_sub(tmp_ap, src_ap, tmp_ap)
            s_ve.inc(i3)
            s_ve.wait(nc.vector)
            i4 = nc.vector.tensor_copy(lo_ap, tmp_ap)
            s_ve.inc(i4)
            return s_ve.v

        p0stack = ExitStack()
        p0rowh = p0stack.enter_context(nc.sbuf_tensor("p0rowh", [1, D], BF16))
        p0rowl = p0stack.enter_context(nc.sbuf_tensor("p0rowl", [1, D], BF16))
        p0rf = p0stack.enter_context(nc.sbuf_tensor("p0rf", [1, D], F32))
        p0bb = p0stack.enter_context(nc.sbuf_tensor("p0bb", [1, D], F32))
        p0rb = p0stack.enter_context(nc.sbuf_tensor("p0rb", [1, 8], F32))
        p0rbh = p0stack.enter_context(nc.sbuf_tensor("p0rbh", [1, 8], BF16))
        p0rbl = p0stack.enter_context(nc.sbuf_tensor("p0rbl", [1, 8], BF16))
        p0xp = p0stack.enter_context(nc.sbuf_tensor("p0xp", [128, 96 * NDT], F32))
        p0xt = p0stack.enter_context(nc.sbuf_tensor("p0xt", [128, 96 * NDT], F32))
        p0dw = p0stack.enter_context(nc.sbuf_tensor("p0dw", [DTR, CH], F32))
        p0dt = p0stack.enter_context(nc.sbuf_tensor("p0dt", [DTR, CH], F32))
        p0ps = p0stack.enter_context(nc.psum_tensor("p0ps", [128, 512], F32))
        ones32 = p0stack.enter_context(nc.sbuf_tensor("ones32", [32, 128], F32))

        for dst, src in ((normw_sb, norm_w), (cb_sb, conv_b), (dtb_sb, dt_b),
                         (d_sb, d_vec)):
            for k in range(NKT):
                i = nc.sync.dma_start(out=dst[:, k:k + 1],
                                      in_=src[k * 128:(k + 1) * 128, :])
                s_cst.inc(i, 16)
        for k in range(NDT):
            i = nc.sync.dma_start(out=cw_sb[:, k * 4:(k + 1) * 4],
                                  in_=conv_w[k * 128:(k + 1) * 128, :])
            s_cst.inc(i, 16)
            i = nc.sync.dma_start(out=a_sb[:, k * NS:(k + 1) * NS],
                                  in_=a_log[k * 128:(k + 1) * 128, :])
            s_cst.inc(i, 16)
            i = nc.sync.dma_start(out=p0xp[:, k * 96:(k + 1) * 96],
                                  in_=xp_w[k * 128:(k + 1) * 128, :])
            s_cst.inc(i, 16)
        i = nc.sync.dma_start(out=ident[:], in_=ident_in)
        s_cst.inc(i, 16)
        i = nc.sync.dma_start(out=p0bb[:], in_=base_b)
        s_cst.inc(i, 16)
        i = nc.sync.dma_start(out=p0rb[:], in_=rout_b)
        s_cst.inc(i, 16)
        i = nc.sync.dma_start(out=p0dw[:], in_=dt_w)
        s_cst.inc(i, 16)
        CST = s_cst.v

        s_cst.wait(nc.vector, CST)
        nc.vector.memset(ones1[:], 1.0)
        nc.vector.memset(consts[:, 0:1], EPS)
        nc.vector.memset(consts[:, 1:2], 1.0)
        nc.const_aps.register(F32, EPS, consts[:, 0:1])
        nc.const_aps.register(F32, 1.0, consts[:, 1:2])
        i = nc.vector.tensor_copy(ident_bf[:], ident[:])
        s_ve.inc(i)
        i = nc.vector.memset(ones32[:], 1.0)
        s_ve.inc(i)
        s_ve.wait(nc.vector)
        for j in range(2 * NS):
            i = nc.vector.tensor_tensor(
                out=selbc[:, j * 128:(j + 1) * 128], in0=ones32[:],
                in1=ident[0:32, j:j + 1].broadcast_to([32, 128]), op=OP.mult)
        s_ve.inc(i)
        s_ve.wait(nc.scalar)
        s_cst.wait(nc.scalar, CST)
        i = nc.scalar.activation(out=a_sb[:], in_=a_sb[:], func=AF.Exp)
        s_sc.inc(i)
        s_sc.wait(nc.vector)
        i = nc.vector.tensor_scalar_mul(a_sb[:], a_sb[:], -1.0)
        s_ve.inc(i)
        split_hilo(p0bb[:], p0rowh[:], p0rowl[:], p0rf[:])
        split_hilo(p0rb[:], p0rbh[:], p0rbl[:], p0rf[:, 0:8])
        split_hilo(p0xp[:], xph[:], xpl[:], p0xt[:])
        p0_split = split_hilo(p0dw[:], dtwh[:], dtwl[:], p0dt[:])
        s_ve.wait(nc.tensor, p0_split)
        for hf in range(2):
            nc.tensor.matmul(p0ps[:, 0:512], ones1[:],
                             p0rowh[:, hf * 512:(hf + 1) * 512], start=True, stop=False)
            i = nc.tensor.matmul(p0ps[:, 0:512], ones1[:],
                                 p0rowl[:, hf * 512:(hf + 1) * 512], start=False, stop=True)
            s_pe.inc(i)
            s_pe.wait(nc.scalar)
            i = nc.scalar.copy(basb[:, hf * 512:(hf + 1) * 512], p0ps[:, 0:512])
            s_sc.inc(i)
            s_sc.wait(nc.tensor)
        nc.tensor.matmul(p0ps[:, 0:8], ones1[:], p0rbh[:], start=True, stop=False)
        i = nc.tensor.matmul(p0ps[:, 0:8], ones1[:], p0rbl[:], start=False, stop=True)
        s_pe.inc(i)
        s_pe.wait(nc.scalar)
        i = nc.scalar.copy(routb[:], p0ps[:, 0:8])
        s_sc.inc(i)
        s_sc.wait(nc.tensor)
        s_sc.wait(nc.sync)
        p0stack.close()

        sc_xn = {}
        rs_v = {}
        pa_ev = {}
        pe_xn = {}
        sc_evac = {}
        last_bank_evac = {}
        conv_done = {}
        usplit_done = {}
        xp_pe = {}
        spill_done = {}
        gidx = 0

        hs = ExitStack()
        with hs:
            xnh = hs.enter_context(nc.sbuf_tensor("xnh", [128, NKT * TH], BF16))
            xnl = hs.enter_context(nc.sbuf_tensor("xnl", [128, NKT * TH], BF16))
            dbc_sb = hs.enter_context(nc.sbuf_tensor("dbcsb", [96, T], F32))
            for h in range(2):
                with ExitStack() as pas:
                    A = lambda n, s, d=F32: pas.enter_context(
                        nc.sbuf_tensor(f"{n}_h{h}", s, d))
                    xgs = [A("xg0", [128, D]), A("xg1", [128, D])]
                    xsqs = [A("xsq0", [128, D]), A("xsq1", [128, D])]
                    rscs = [A("rsc0", [128, 4]), A("rsc1", [128, 4])]
                    xnf = A("xnf", [128, NKT * 512])
                    xnt = A("xnt", [128, 512])
                    pts = [pas.enter_context(nc.psum_tensor(f"pt0_h{h}", [128, 128], F32)),
                           pas.enter_context(nc.psum_tensor(f"pt1_h{h}", [128, 128], F32))]
                    for g in range(NG):
                        slot = gidx % 2
                        xgt = xgs[slot]
                        if gidx >= 2:
                            s_ve.wait(nc.sync, sc_xn[gidx - 2])
                            s_pe.wait(nc.sync, pe_xn[gidx - 2])
                        if h == 1 and g == 0:
                            s_pe.wait(nc.sync)
                            s_ve.wait(nc.sync)
                            s_sc.wait(nc.sync)
                            s_gf.wait(nc.sync, spill_done[NDT - 1])
                            s_gf.wait(nc.vector, spill_done[NDT - 1])
                            s_gf.wait(nc.scalar, spill_done[NDT - 1])
                        xsq = xsqs[slot]
                        rsc = rscs[slot]
                        i = nc.sync.dma_start(
                            out=xgt[:], in_=xb[h * TH + g * 128: h * TH + (g + 1) * 128, :])
                        gl = s_xg[slot].inc(i, 16)
                        s_xg[slot].wait(nc.scalar, gl)
                        if gidx >= 2 and (h == 0 or g >= 2):
                            s_ve.wait(nc.scalar, rs_v[gidx - 2])
                            s_ve.wait(nc.scalar, sc_xn[gidx - 2])
                        i = nc.scalar.activation(out=xsq[:], in_=xgt[:], func=AF.Square)
                        sq_c = s_sc.inc(i)
                        s_sc.wait(nc.vector, sq_c)
                        i = nc.vector.reduce_sum(rsc[:, 0:1], xsq[:], axis=AX.X)
                        rs_v[gidx] = s_ve.inc(i)
                        s_ve.wait(nc.scalar, rs_v[gidx])
                        i = nc.scalar.activation(out=rsc[:, 1:2], in_=rsc[:, 0:1],
                                                 func=AF.Ln, scale=1.0 / D, bias=EPS)
                        s_sc.inc(i)
                        s_sc.wait(nc.scalar)
                        i = nc.scalar.activation(out=rsc[:, 2:3], in_=rsc[:, 1:2],
                                                 func=AF.Exp, scale=-0.5)
                        rs2_c = s_sc.inc(i)
                        s_sc.wait(nc.vector, rs2_c)
                        s_xg[slot].wait(nc.vector, gl)
                        i = nc.vector.tensor_tensor(
                            out=xgt[:], in0=xgt[:],
                            in1=rsc[:, 2:3].broadcast_to([128, D]), op=OP.mult)
                        sc_xn[gidx] = s_ve.inc(i)
                        s_ve.wait(nc.tensor, sc_xn[gidx])
                        for k in range(NKT):
                            if k >= 2:
                                s_ve.wait(nc.tensor, pa_ev[(gidx, k - 2)])
                            i = nc.tensor.transpose(pts[k % 2][:, 0:128],
                                                    xgt[:, k * 128:(k + 1) * 128],
                                                    ident[:])
                            pe_xn[gidx] = s_pe.inc(i)
                            s_pe.wait(nc.vector, pe_xn[gidx])
                            i = nc.vector.tensor_scalar(
                                out=xnf[:, k * 512 + (g % 4) * 128:
                                        k * 512 + (g % 4 + 1) * 128],
                                in0=pts[k % 2][:, 0:128], scalar1=normw_sb[:, k:k + 1],
                                scalar2=None, op0=OP.mult)
                            pa_ev[(gidx, k)] = s_ve.inc(i)
                        gidx += 1
                        if g % 4 == 3:
                            gb = g // 4
                            if taps:
                                s_ve.wait(nc.gpsimd)
                                for k in range(NKT):
                                    i = nc.gpsimd.dma_start(
                                        out=tap_xn[k * 128:(k + 1) * 128,
                                                   h * TH + gb * 512:
                                                   h * TH + (gb + 1) * 512],
                                        in_=xnf[:, k * 512:(k + 1) * 512])
                                    s_sp.inc(i, 16)
                                gfence(nc.scalar, nc.vector)
                            for k in range(NKT):
                                split_hilo(
                                    xnf[:, k * 512:(k + 1) * 512],
                                    xnh[:, k * TH + gb * 512: k * TH + (gb + 1) * 512],
                                    xnl[:, k * TH + gb * 512: k * TH + (gb + 1) * 512],
                                    xnt[:])
                            s_sc.wait(nc.vector)
                with ExitStack() as l1s:
                    A = lambda n, s, d=F32: l1s.enter_context(
                        nc.sbuf_tensor(f"{n}_h{h}", s, d))
                    P = lambda n, s: l1s.enter_context(
                        nc.psum_tensor(f"{n}_h{h}", s, F32))
                    wufs = [A(f"wuf{j}", [128, 512]) for j in range(4)]
                    wtmp = A("wtmp", [128, 512])
                    whs = [(A("wh0", [128, 2048], BF16), A("wl0", [128, 2048], BF16)),
                           (A("wh1", [128, 2048], BF16), A("wl1", [128, 2048], BF16))]
                    uraws = [A("uraw0", [128, TH + 3]), A("uraw1", [128, TH + 3])]
                    ucvs = [A("ucv0", [128, TH])]
                    uhl0 = A("uhl0", [128, TH], BF16)
                    ull0 = A("ull0", [128, TH], BF16)
                    ress = [A("res0", [128, TH]), A("res1", [128, TH])]
                    l1tmp = wtmp
                    pus = [P("pu0", [128, 512]), P("pu1", [128, 512])]
                    prs = [P("pr0", [128, 512]), P("pr1", [128, 512])]
                    pdbcs = [P(f"pdbc{j}", [96, 512]) for j in range(NTT)]
                    wsplit = {}
                    if h == 0:
                        last_bank_evac.clear()

                    def load_w_dma(dt):
                        s_ve.wait(nc.sync)
                        if not (h == 0 and dt == 0):
                            s_pe.wait(nc.sync)
                        slot = dt % 2
                        for ch in range(4):
                            src = w_in_u if ch < 2 else w_in_r
                            for k4 in range(4):
                                k = (ch % 2) * 4 + k4
                                i = nc.sync.dma_start(
                                    out=wufs[ch][:, k4 * 128:(k4 + 1) * 128],
                                    in_=src[k * 128:(k + 1) * 128,
                                            dt * 128:(dt + 1) * 128])
                                s_w[slot].inc(i, 16)

                    def load_w_split(dt):
                        slot = dt % 2
                        whd, wld = whs[slot]
                        for ch in range(4):
                            coff = (0 if ch < 2 else 1024) + (ch % 2) * 512
                            wsplit[dt] = split_hilo(
                                wufs[ch][:], whd[:, coff:coff + 512],
                                wld[:, coff:coff + 512], wtmp[:],
                                wait_s=([(s_w[slot], s_w[slot].v)] if ch == 0 else ()))

                    def xproj(dt):
                        gdt = h * NDT + dt
                        s_ve.wait(nc.tensor, usplit_done[gdt])
                        ilast = None
                        for tt in range(NTT):
                            for (wa, xa) in ((xph, uhl0), (xph, ull0), (xpl, uhl0)):
                                ilast = nc.tensor.matmul(
                                    pdbcs[tt][:, 0:512],
                                    wa[:, dt * 96:(dt + 1) * 96],
                                    xa[:, tt * 512:(tt + 1) * 512],
                                    start=(dt == 0 and wa is xph and xa is uhl0),
                                    stop=(dt == NDT - 1 and wa is xpl))
                            s_pe.inc(ilast)
                        xp_pe[gdt] = s_pe.v

                    load_w_dma(0)
                    load_w_split(0)
                    if NDT > 1:
                        load_w_dma(1)
                    for dt in range(NDT):
                        whd, wld = whs[dt % 2]
                        uraw = uraws[dt % 2]
                        ucv = ucvs[0]
                        resb = ress[dt % 2]
                        key2 = (h * NDT + dt) - 2
                        for cs in range(2):
                            for tt in range(NTT):
                                ps = (pus if cs == 0 else prs)[tt % 2]
                                gk = h * NDT * 2 * NTT + dt * 2 * NTT + cs * NTT + tt
                                bank = (cs, tt % 2)
                                if bank in last_bank_evac:
                                    s_sc.wait(nc.tensor, last_bank_evac[bank])
                                s_ve.wait(nc.tensor, wsplit[dt])
                                first = True
                                ilast = None
                                for (wa, xa) in ((whd, xnh), (whd, xnl), (wld, xnh)):
                                    for k in range(NKT):
                                        ilast = nc.tensor.matmul(
                                            ps[:, 0:512],
                                            wa[:, cs * 1024 + k * 128: cs * 1024 + (k + 1) * 128],
                                            xa[:, k * TH + tt * 512: k * TH + (tt + 1) * 512],
                                            start=first,
                                            stop=(wa is wld and k == NKT - 1))
                                        first = False
                                pe_mm = s_pe.inc(ilast)
                                s_pe.wait(nc.scalar, pe_mm)
                                if cs == 0:
                                    if key2 >= 0 and tt == 0:
                                        s_ve.wait(nc.scalar, conv_done[key2])
                                    i = nc.scalar.copy(
                                        uraw[:, 3 + tt * 512: 3 + (tt + 1) * 512],
                                        ps[:, 0:512])
                                else:
                                    if key2 >= 0 and tt == 0:
                                        s_gf.wait(nc.scalar, spill_done[key2])
                                    i = nc.scalar.activation(
                                        out=resb[:, tt * 512:(tt + 1) * 512],
                                        in_=ps[:, 0:512], func=AF.Silu)
                                sc_evac[gk] = s_sc.inc(i)
                                last_bank_evac[bank] = sc_evac[gk]
                        if dt + 1 < NDT:
                            load_w_split(dt + 1)
                        if dt + 2 < NDT:
                            load_w_dma(dt + 2)
                        if dt >= 1:
                            xproj(dt - 1)
                        s_sc.wait(nc.vector)
                        if h == 0:
                            i = nc.vector.memset(uraw[:, 0:3], 0.0)
                        else:
                            i = nc.vector.tensor_copy(uraw[:, 0:3],
                                                      tails[:, dt * 3:(dt + 1) * 3])
                        s_ve.inc(i)
                        key1 = (h * NDT + dt) - 1
                        if key1 >= 0:
                            s_gf.wait(nc.vector, spill_done[key1])
                        s_ve.wait(nc.vector)
                        i = nc.vector.tensor_scalar(out=ucv[:], in0=uraw[:, 0:TH],
                                                    scalar1=cw_sb[:, dt * 4:dt * 4 + 1],
                                                    scalar2=None, op0=OP.mult)
                        s_ve.inc(i)
                        for k in range(1, 4):
                            s_ve.wait(nc.vector)
                            i = nc.vector.scalar_tensor_tensor(
                                out=ucv[:], in0=uraw[:, k:k + TH],
                                scalar=cw_sb[:, dt * 4 + k:dt * 4 + k + 1],
                                in1=ucv[:], op0=OP.mult, op1=OP.add)
                            if k < 3 or h == 0:
                                s_ve.inc(i)
                        if h == 0:
                            i = nc.vector.tensor_copy(tails[:, dt * 3:(dt + 1) * 3],
                                                      uraw[:, TH:TH + 3])
                        conv_done[h * NDT + dt] = s_ve.inc(i)
                        s_ve.wait(nc.scalar, conv_done[h * NDT + dt])
                        i = nc.scalar.activation(out=ucv[:], in_=ucv[:], func=AF.Silu,
                                                 bias=cb_sb[:, dt:dt + 1])
                        silu_c = s_sc.inc(i)
                        s_sc.wait(nc.gpsimd, silu_c)
                        i = nc.gpsimd.dma_start(out=u_scr[dt * 128:(dt + 1) * 128,
                                                          h * TH:(h + 1) * TH], in_=ucv[:])
                        s_sp.inc(i, 16)
                        i = nc.gpsimd.dma_start(out=res_scr[dt * 128:(dt + 1) * 128,
                                                            h * TH:(h + 1) * TH], in_=resb[:])
                        s_sp.inc(i, 16)
                        if taps:
                            i = nc.gpsimd.dma_start(
                                out=tap_u[dt * 128:(dt + 1) * 128, h * TH:(h + 1) * TH],
                                in_=ucv[:])
                            s_sp.inc(i, 16)
                            i = nc.gpsimd.dma_start(
                                out=tap_res[dt * 128:(dt + 1) * 128, h * TH:(h + 1) * TH],
                                in_=resb[:])
                            s_sp.inc(i, 16)
                        spill_done[h * NDT + dt] = gfence()
                        gdt = h * NDT + dt
                        ws0 = [(s_sc, silu_c)]
                        wv0 = []
                        if gdt >= 1:
                            ws0.append((s_pe, xp_pe[gdt - 1]))
                            wv0.append((s_pe, xp_pe[gdt - 1]))
                        for uc in range(TH // 512):
                            usplit_done[gdt] = split_hilo(
                                ucv[:, uc * 512:(uc + 1) * 512],
                                uhl0[:, uc * 512:(uc + 1) * 512],
                                ull0[:, uc * 512:(uc + 1) * 512], l1tmp[:],
                                wait_s=ws0 if uc == 0 else (),
                                wait_v=wv0 if uc == 0 else ())
                    xproj(NDT - 1)
                    s_pe.wait(nc.vector)
                    for tt in range(NTT):
                        i = nc.vector.tensor_copy(
                            dbc_sb[:, h * TH + tt * 512: h * TH + (tt + 1) * 512],
                            pdbcs[tt][0:96, 0:512])
                        s_ve.inc(i)

            s_ve.wait(nc.gpsimd)
            i = nc.gpsimd.dma_start(out=dbcp, in_=dbc_sb[:])
            s_sp.inc(i, 16)
            nc.gpsimd.wait_ge(s_sp.h, s_sp.v)
            i = nc.gpsimd.collective_compute("AllReduce", OP.add, replica_groups=PAIRS,
                                             ins=[dbcp], outs=[dbcs])
            s_cc.inc(i)
            s_cc.wait(nc.sync)
            with nc.sbuf_tensor("ddt_sb", [DTR, T], F32) as ddt_sb, \
                 nc.sbuf_tensor("dbc2_sb", [32, T], F32) as dbc2_sb, \
                 nc.sbuf_tensor("p2t", [DTR, T], F32) as p2t:
                i = nc.sync.dma_start(out=ddt_sb[:], in_=dbcs[0:DTR, :])
                s_db.inc(i, 16)
                i = nc.sync.dma_start(out=dbc2_sb[:], in_=dbcs[DTR:96, :])
                dbc_ld = s_db.inc(i, 16)
                if taps:
                    s_db.wait(nc.gpsimd, dbc_ld)
                    i = nc.gpsimd.dma_start(out=tap_dbc[0:DTR, :], in_=ddt_sb[:])
                    s_sp.inc(i, 16)
                    i = nc.gpsimd.dma_start(out=tap_dbc[DTR:96, :], in_=dbc2_sb[:])
                    s_sp.inc(i, 16)
                split_hilo(ddt_sb[:], dth[:], dtl[:], p2t[:],
                           wait_s=[(s_db, dbc_ld)])
                p2_done = split_hilo(dbc2_sb[:], bchi[:], bclo[:], p2t[0:32, :])
                if taps:
                    gfence(nc.sync)

        with ExitStack() as p3s:
            A = lambda n, s, d=F32: p3s.enter_context(nc.sbuf_tensor(n, s, d))
            P = lambda n, s: p3s.enter_context(nc.psum_tensor(n, s, F32))
            delta = A("delta", [128, NDT * 512])
            delu = A("delu", [128, NDT * 512])
            ybuf = A("ybuf", [128, NDT * 512])
            urels = [A("urel0", [128, NDT * 512]), A("urel1", [128, NDT * 512])]
            resrel = A("resrel", [128, NDT * 512])
            hw = A("hw", [128, NDT * 512])
            prodb = A("prodb", [128, 512])
            etmps = [A("etmp0", [128, 512]), A("etmp1", [128, 512])]
            dAs = [A(f"dA{j}", [128, 512]) for j in range(3)]
            dBus = [A(f"dBu{j}", [128, 512]) for j in range(3)]
            brss = [A("brs0", [128, 512]), A("brs1", [128, 512])]
            crss = [A("crs0", [128, 512]), A("crs1", [128, 512])]
            yh_t = A("yh_t", [128, NDT * 512], BF16)
            yl_t = A("yl_t", [128, NDT * 512], BF16)
            p3tmp = prodb
            pds = [P("pd0", [128, 512]), P("pd1", [128, 512])]
            pbs = [P("pb0", [128, 512]), P("pb1", [128, 512])]
            pcs = [P("pc0", [128, 512]), P("pc1", [128, 512])]
            s_ve.wait(nc.vector, p2_done)
            i = nc.vector.memset(state[:], 0.0)
            s_ve.inc(i)
            l1_fence = spill_done[2 * NDT - 1]
            s_gf.wait(nc.sync, l1_fence)
            s_gf.wait(nc.scalar, l1_fence)
            for dt in range(NDT):
                i = nc.sync.dma_start(out=urels[0][:, dt * 512:(dt + 1) * 512],
                                      in_=u_scr[dt * 128:(dt + 1) * 128, 0:512])
                s_ur[0].inc(i, 16)
            u_ld = {0: (0, s_ur[0].v)}
            niter = 0
            sc_exp_done = {}
            v_add1_done = {}
            sBC_grp = {}
            v_ypost = {}
            sc_ysplit_done = {}
            p3_fence = {}
            v_yinit_last = 0
            for tc in range(NTC):
                uslot = tc % 2
                urel = urels[uslot]
                if tc + 1 < NTC:
                    if tc >= 1:
                        s_ve.wait(nc.sync, v_yinit_last)
                    nslot = (tc + 1) % 2
                    for dt in range(NDT):
                        i = nc.sync.dma_start(
                            out=urels[nslot][:, dt * 512:(dt + 1) * 512],
                            in_=u_scr[dt * 128:(dt + 1) * 128,
                                      (tc + 1) * 512:(tc + 2) * 512])
                        s_ur[nslot].inc(i, 16)
                    u_ld[tc + 1] = (nslot, s_ur[nslot].v)
                if tc >= 1:
                    s_ve.wait(nc.scalar, v_ypost[tc - 1])
                for dt in range(NDT):
                    i = nc.scalar.dma_start(out=resrel[:, dt * 512:(dt + 1) * 512],
                                            in_=res_scr[dt * 128:(dt + 1) * 128,
                                                        tc * 512:(tc + 1) * 512])
                    s_rr.inc(i, 16)
                res_ld = s_rr.v
                if tc >= 1:
                    s_sc.wait(nc.vector, sc_ysplit_done[tc - 1])
                    s_gB.wait(nc.vector, NDT * NS * tc)
                    if taps:
                        s_gf.wait(nc.vector, p3_fence[tc - 1])
                        s_gf.wait(nc.scalar, p3_fence[tc - 1])
                for dt in range(NDT):
                    ps = pds[dt % 2]
                    gdt = tc * NDT + dt
                    if gdt >= 2:
                        s_sc.wait(nc.tensor, sc_exp_done[gdt - 2])
                    elif gdt == 0:
                        s_ve.wait(nc.tensor, p2_done)
                    first = True
                    ilast = None
                    for (wa, xa) in ((dtwh, dth), (dtwh, dtl), (dtwl, dth)):
                        ilast = nc.tensor.matmul(ps[:, 0:512],
                                                 wa[:, dt * 128:(dt + 1) * 128],
                                                 xa[:, tc * 512:(tc + 1) * 512],
                                                 start=first, stop=(wa is dtwl))
                        first = False
                    pe_d = s_pe.inc(ilast)
                    s_pe.wait(nc.scalar, pe_d)
                    et = etmps[dt % 2]
                    if gdt >= 2:
                        s_ve.wait(nc.scalar, v_add1_done[gdt - 2])
                    i = nc.scalar.activation(out=et[:], in_=ps[:, 0:512], func=AF.Exp,
                                             bias=dtb_sb[:, dt:dt + 1])
                    sc_exp_done[gdt] = s_sc.inc(i)
                    s_sc.wait(nc.vector, sc_exp_done[gdt])
                    i = nc.vector.tensor_scalar_add(et[:], et[:], 1.0)
                    v_add1_done[gdt] = s_ve.inc(i)
                    s_ve.wait(nc.scalar, v_add1_done[gdt])
                    i = nc.scalar.activation(out=delta[:, dt * 512:(dt + 1) * 512],
                                             in_=et[:], func=AF.Ln)
                    sc_dl = s_sc.inc(i)
                    s_ur[u_ld[tc][0]].wait(nc.vector, u_ld[tc][1])
                    s_sc.wait(nc.vector, sc_dl)
                    i = nc.vector.tensor_mul(delu[:, dt * 512:(dt + 1) * 512],
                                             delta[:, dt * 512:(dt + 1) * 512],
                                             urel[:, dt * 512:(dt + 1) * 512])
                    s_ve.inc(i)
                    i = nc.vector.tensor_scalar(
                        out=ybuf[:, dt * 512:(dt + 1) * 512],
                        in0=urel[:, dt * 512:(dt + 1) * 512],
                        scalar1=d_sb[:, dt:dt + 1], scalar2=None, op0=OP.mult)
                    v_yinit_last = s_ve.inc(i)
                delu_done = s_ve.v
                if taps:
                    s_ve.wait(nc.gpsimd, delu_done)
                    for dt in range(NDT):
                        i = nc.gpsimd.dma_start(
                            out=tap_delta[dt * 128:(dt + 1) * 128, tc * 512:(tc + 1) * 512],
                            in_=delta[:, dt * 512:(dt + 1) * 512])
                        s_sp.inc(i, 16)
                for s in range(NS):
                    grp = tc * NS + s
                    pbx, pcx = pbs[grp % 2], pcs[grp % 2]
                    if grp >= 2:
                        s_sBC.wait(nc.tensor, sBC_grp[grp - 2])
                    selb_ap = selbc[:, s * 128:(s + 1) * 128]
                    selc_ap = selbc[:, (NS + s) * 128:(NS + s + 1) * 128]
                    nc.tensor.matmul(pbx[:, 0:512], selb_ap,
                                     bchi[:, tc * 512:(tc + 1) * 512], start=True, stop=False)
                    nc.tensor.matmul(pbx[:, 0:512], selb_ap,
                                     bclo[:, tc * 512:(tc + 1) * 512], start=False, stop=True)
                    nc.tensor.matmul(pcx[:, 0:512], selc_ap,
                                     bchi[:, tc * 512:(tc + 1) * 512], start=True, stop=False)
                    i = nc.tensor.matmul(pcx[:, 0:512], selc_ap,
                                         bclo[:, tc * 512:(tc + 1) * 512], start=False, stop=True)
                    pe_bc = s_pe.inc(i)
                    brx, crx = brss[grp % 2], crss[grp % 2]
                    s_pe.wait(nc.scalar, pe_bc)
                    if grp >= 2:
                        s_gB.wait(nc.scalar, NDT * (grp - 1))
                        s_vP.wait(nc.scalar, NDT * (grp - 1))
                    nc.scalar.copy(brx[:], pbx[:, 0:512])
                    i = nc.scalar.copy(crx[:], pcx[:, 0:512])
                    sBC_grp[grp] = s_sBC.inc(i)
                    for s2 in range(NDT):
                        dt = s2
                        n = niter
                        dAx = dAs[n % 3]
                        dBx = dBus[n % 3]
                        if n >= 3:
                            s_vS.wait(nc.scalar, n - 2)
                        i = nc.scalar.activation(
                            out=dAx[:], in_=delta[:, dt * 512:(dt + 1) * 512],
                            func=AF.Exp,
                            scale=a_sb[:, dt * NS + s:dt * NS + s + 1])
                        s_sA.inc(i)
                        if n >= 3:
                            s_vS.wait(nc.gpsimd, n - 2)
                        if dt == 0:
                            s_sBC.wait(nc.gpsimd, sBC_grp[grp])
                            s_ve.wait(nc.gpsimd, delu_done)
                        i = nc.gpsimd.tensor_mul(dBx[:],
                                                 delu[:, dt * 512:(dt + 1) * 512],
                                                 brx[:])
                        s_gB.inc(i)
                        s_sA.wait(nc.vector, n + 1)
                        s_gB.wait(nc.vector, n + 1)
                        if dt == 0:
                            s_sBC.wait(nc.vector, sBC_grp[grp])
                            s_ve.wait(nc.vector)
                        col = dt * NS + s
                        i = nc.vector.tensor_tensor_scan(
                            out=hw[:, dt * 512:(dt + 1) * 512],
                            data0=dAx[:], data1=dBx[:],
                            initial=state[:, col:col + 1],
                            op0=OP.mult, op1=OP.add)
                        s_vS.inc(i)
                        s_vS.wait(nc.vector, n + 1)
                        i = nc.vector.tensor_mul(prodb[:],
                                                 hw[:, dt * 512:(dt + 1) * 512], crx[:])
                        s_vP.inc(i)
                        s_vP.wait(nc.vector, n + 1)
                        i = nc.vector.tensor_add(ybuf[:, dt * 512:(dt + 1) * 512],
                                                 ybuf[:, dt * 512:(dt + 1) * 512],
                                                 prodb[:])
                        s_ve.inc(i)
                        niter += 1
                    s_ve.wait(nc.vector)
                    i = nc.vector.tensor_copy(
                        state[:].rearrange("p (d s2) -> p d s2", s2=NS)[:, :, s],
                        hw[:].rearrange("p (d t) -> p d t", t=512)[:, :, 511])
                    s_ve.inc(i)
                s_rr.wait(nc.vector, res_ld)
                s_ve.wait(nc.vector)
                for dt in range(NDT):
                    i = nc.vector.tensor_mul(ybuf[:, dt * 512:(dt + 1) * 512],
                                             ybuf[:, dt * 512:(dt + 1) * 512],
                                             resrel[:, dt * 512:(dt + 1) * 512])
                    v_ypost[tc] = s_ve.inc(i)
                    if taps:
                        s_ve.wait(nc.gpsimd, v_ypost[tc])
                        i = nc.gpsimd.dma_start(
                            out=tap_y[dt * 128:(dt + 1) * 128, tc * 512:(tc + 1) * 512],
                            in_=ybuf[:, dt * 512:(dt + 1) * 512])
                        s_sp.inc(i, 16)
                    ws = [(s_ve, v_ypost[tc])]
                    if tc >= 1:
                        ws.append((s_gf, p3_fence[tc - 1]))
                    split_hilo(ybuf[:, dt * 512:(dt + 1) * 512],
                               yh_t[:, dt * 512:(dt + 1) * 512],
                               yl_t[:, dt * 512:(dt + 1) * 512],
                               p3tmp[:], wait_s=ws)
                    ysplit = s_ve.v
                    s_ve.wait(nc.gpsimd, ysplit)
                    i = nc.gpsimd.dma_start(out=yhi_scr[dt * 128:(dt + 1) * 128,
                                                        tc * 512:(tc + 1) * 512],
                                            in_=yh_t[:, dt * 512:(dt + 1) * 512])
                    s_sp.inc(i, 16)
                    i = nc.gpsimd.dma_start(out=ylo_scr[dt * 128:(dt + 1) * 128,
                                                        tc * 512:(tc + 1) * 512],
                                            in_=yl_t[:, dt * 512:(dt + 1) * 512])
                    s_sp.inc(i, 16)
                sc_ysplit_done[tc] = s_sc.v
                p3_fence[tc] = gfence()
            p3_last_fence = p3_fence[NTC - 1]

        cc_val = {}
        with ExitStack() as p4s:
            A = lambda n, s, d=F32: p4s.enter_context(nc.sbuf_tensor(n, s, d))
            woh = A("woh", [128, NDT * 1024], BF16)
            wol = A("wol", [128, NDT * 1024], BF16)
            yhrs = [(A("yhr0", [128, NDT * 512], BF16), A("ylr0", [128, NDT * 512], BF16)),
                    (A("yhr1", [128, NDT * 512], BF16), A("ylr1", [128, NDT * 512], BF16))]
            posb = [A(f"posb{j}", [128, 512]) for j in range(NDT)]
            pos = [p4s.enter_context(nc.psum_tensor(f"po{j}", [128, 512], F32))
                   for j in range(3)]
            s_pe.wait(nc.sync)
            s_sc.wait(nc.sync)
            s_ve.wait(nc.sync)
            s_gf.wait(nc.sync, p3_last_fence)
            with nc.sbuf_tensor("woS", [128, NDT * 1024], F32) as woS, \
                 nc.sbuf_tensor("woT", [128, 1024], F32) as woT:
                for dt in range(NDT):
                    i = nc.sync.dma_start(out=woS[:, dt * 1024:(dt + 1) * 1024],
                                          in_=w_out[dt * 128:(dt + 1) * 128, :])
                    s_wo.inc(i, 16)
                wo_ld = s_wo.v
                for dt in range(NDT):
                    wo_done = split_hilo(woS[:, dt * 1024:(dt + 1) * 1024],
                                         woh[:, dt * 1024:(dt + 1) * 1024],
                                         wol[:, dt * 1024:(dt + 1) * 1024],
                                         woT[:], wait_s=[(s_wo, wo_ld)])
            v_oev = {}
            pe_o_done = {}
            gf_p4 = {}
            p4_bank_evac = {}
            p4_order = []
            for j in range(NTC // 2):
                p4_order += [j, j + NTC // 2]
            for pp, tc in enumerate(p4_order):
                slot = pp % 2
                yhr, ylr = yhrs[slot]
                if pp >= 2:
                    s_pe.wait(nc.sync, pe_o_done[p4_order[pp - 2]])
                for dt in range(NDT):
                    i = nc.sync.dma_start(out=yhr[:, dt * 512:(dt + 1) * 512],
                                          in_=yhi_scr[dt * 128:(dt + 1) * 128,
                                                      tc * 512:(tc + 1) * 512])
                    s_yr[slot].inc(i, 16)
                    i = nc.sync.dma_start(out=ylr[:, dt * 512:(dt + 1) * 512],
                                          in_=ylo_scr[dt * 128:(dt + 1) * 128,
                                                      tc * 512:(tc + 1) * 512])
                    s_yr[slot].inc(i, 16)
                y_ld = s_yr[slot].v
                half, col = tc // (NTC // 2), (tc % (NTC // 2)) * 512
                for m in range(NDT):
                    ps = pos[m % 3]
                    mi = tc * NDT + m
                    if m == 0:
                        s_yr[slot].wait(nc.tensor, y_ld)
                        if tc == 0:
                            s_ve.wait(nc.tensor, wo_done)
                    if m % 3 in p4_bank_evac:
                        s_ve.wait(nc.tensor, p4_bank_evac[m % 3])
                    first = True
                    ilast = None
                    for (wa, ya) in ((woh, yhr), (woh, ylr), (wol, yhr)):
                        for dt in range(NDT):
                            ilast = nc.tensor.matmul(
                                ps[:, 0:512],
                                wa[:, dt * 1024 + m * 128: dt * 1024 + (m + 1) * 128],
                                ya[:, dt * 512:(dt + 1) * 512],
                                start=first, stop=(wa is wol and dt == NDT - 1))
                            first = False
                    pe_m = s_pe.inc(ilast)
                    pe_o_done[tc] = pe_m
                    s_pe.wait(nc.vector, pe_m)
                    if pp >= 1:
                        s_gf.wait(nc.vector, gf_p4[p4_order[pp - 1]])
                    i = nc.vector.tensor_copy(posb[m][:], ps[:, 0:512])
                    v_oev[mi] = s_ve.inc(i)
                    p4_bank_evac[m % 3] = v_oev[mi]
                    s_ve.wait(nc.gpsimd, v_oev[mi])
                    blk = tc % (NTC // 2)
                    r0 = blk * 2 * D + half * D + m * 128
                    i = nc.gpsimd.dma_start(
                        out=part_scr[r0:r0 + 128, :],
                        in_=posb[m][:])
                    s_sp.inc(i, 16)
                gf_p4[tc] = gfence()
                if pp % 2 == 1:
                    j = tc % (NTC // 2)
                    i = nc.gpsimd.collective_compute(
                        "ReduceScatter", OP.add, replica_groups=PAIRS,
                        ins=[part_scr[j * 2 * D:(j + 1) * 2 * D, :]],
                        outs=[mix_scr[j * D:(j + 1) * D, :]])
                    cc_val[j] = s_cc.inc(i)

        if taps:
            s_cc.wait(nc.gpsimd)
            with nc.sbuf_tensor("tapm", [128, 512], F32) as tapm:
                for j in range(TH // 512):
                    for k in range(NKT):
                        i = nc.gpsimd.dma_start(
                            out=tapm[:],
                            in_=mix_scr[j * D + k * 128: j * D + (k + 1) * 128, :])
                        s_sp.inc(i, 16)
                        nc.gpsimd.wait_ge(s_sp.h, s_sp.v)
                        i = nc.gpsimd.dma_start(
                            out=tap_mix[k * 128:(k + 1) * 128, j * 512:(j + 1) * 512],
                            in_=tapm[:])
                        s_sp.inc(i, 16)
                        nc.gpsimd.wait_ge(s_sp.h, s_sp.v)
            gfence(nc.sync, nc.scalar, nc.vector, nc.tensor)

        with ExitStack() as p6s:
            A = lambda n, s, d=F32: p6s.enter_context(nc.sbuf_tensor(n, s, d))
            P = lambda n, s: p6s.enter_context(nc.psum_tensor(n, s, F32))
            bwh = A("bwh", [128, NKT * 1024], BF16)
            bwl = A("bwl", [128, NKT * 1024], BF16)
            lah = A("lah", [128, 8 * NKT], BF16)
            lal = A("lal", [128, 8 * NKT], BF16)
            rwh = A("rwh", [128, 8 * NKT], BF16)
            rwl = A("rwl", [128, 8 * NKT], BF16)
            lbh = A("lbh", [64, D], BF16)
            lbl = A("lbl", [64, D], BF16)
            xhts = [A("xht0", [128, D]), A("xht1", [128, D])]
            mixts = [A("mixt0", [128, NKT * 128]), A("mixt1", [128, NKT * 128])]
            s1 = A("s1", [128, D])
            lnc = A("lnc", [128, 8])
            x2tm = A("x2tm", [128, 4 * D])
            x2f = A("x2f", [128, NKT * 512])
            x2fh = A("x2fh", [128, NKT * 512], BF16)
            x2fl = A("x2fl", [128, NKT * 512], BF16)
            gsm = A("gsm", [128, 48])
            gbf = A("gbf", [128, 8], BF16)
            wmid = A("wmid", [128, 64])
            wmh = A("wmh", [128, 64], BF16)
            wml = A("wml", [128, 64], BF16)
            wmTh = A("wmTh", [64, 128], BF16)
            wmTl = A("wmTl", [64, 128], BF16)
            ffn1 = A("ffn1", [128, D])
            outts = [A("outt0", [128, D]), A("outt1", [128, D])]
            p6tmp = A("p6tmp", [128, 1024])
            pt6s = [P("pt60", [128, 128]), P("pt61", [128, 128])]
            pfs = [P("pf0", [128, 512]), P("pf1", [128, 512])]
            pr8 = P("pr8", [128, 8])
            pm8 = P("pm8", [128, 8])
            pw6 = p6s.enter_context(nc.psum_tensor("pw6", [64, 128], BF16))
            s_pe.wait(nc.sync)
            s_sc.wait(nc.sync)
            s_ve.wait(nc.sync)
            with nc.sbuf_tensor("bwS", [128, NKT * 1024], F32) as bwS, \
                 nc.sbuf_tensor("p6sm", [128, 128], F32) as p6sm:
                p6lb = x2tm[0:64, 0:D]
                for k in range(NKT):
                    i = nc.sync.dma_start(out=bwS[:, k * 1024:(k + 1) * 1024],
                                          in_=base_wt[k * 128:(k + 1) * 128, :])
                    s_bw.inc(i, 16)
                for k in range(NKT):
                    i = nc.sync.dma_start(out=p6sm[:, k * 8:(k + 1) * 8],
                                          in_=lora_a[k * 128:(k + 1) * 128, :])
                    s_bw.inc(i, 16)
                    i = nc.sync.dma_start(out=p6sm[:, 64 + k * 8:64 + (k + 1) * 8],
                                          in_=rout_wt[k * 128:(k + 1) * 128, :])
                    s_bw.inc(i, 16)
                i = nc.sync.dma_start(out=p6lb, in_=lora_b2)
                s_bw.inc(i, 16)
                all_ld = s_bw.v
                for k in range(NKT):
                    split_hilo(bwS[:, k * 1024:(k + 1) * 1024],
                               bwh[:, k * 1024:(k + 1) * 1024],
                               bwl[:, k * 1024:(k + 1) * 1024],
                               p6tmp[:], wait_s=[(s_bw, all_ld)])
                split_hilo(p6sm[:, 0:64], lah[:], lal[:], p6tmp[:, 0:64])
                split_hilo(p6sm[:, 64:128], rwh[:], rwl[:], p6tmp[:, 0:64])
                split_hilo(p6lb, lbh[:], lbl[:], p6tmp[0:64, 0:D])
            pe_base_done = {0: 0}
            st_sems = [s_st, s_stb]
            st_tot = {}
            v_mixadd = {}
            v_x2m = {}
            x2cp = {}
            mx_sems = [s_mx, s_mxb]
            xh_sems = [s_xh, s_xhb]
            mx_tot = {}
            xh_tot = {}
            NTI = NFG * 4

            def p6_prefetch(ti):
                slot = ti % 2
                tok0 = (ti // 4) * 512 + (ti % 4) * 128
                s_cc.wait(nc.sync, cc_val[ti // 4])
                s_cc.wait(nc.scalar, cc_val[ti // 4])
                if ti >= 2:
                    s_ve.wait(nc.sync, v_mixadd[(ti - 2, NKT - 1)])
                    s_ve.wait(nc.scalar, v_mixadd[(ti - 2, NKT - 1)])
                i = nc.sync.dma_start(out=xhts[slot][:], in_=xh[tok0:tok0 + 128, :])
                xh_sems[slot].inc(i, 16)
                xh_tot[ti] = xh_sems[slot].v
                for k in range(NKT):
                    blk, bc = tok0 // 512, tok0 % 512
                    i = nc.scalar.dma_start(
                        out=mixts[slot][:, k * 128:(k + 1) * 128],
                        in_=mix_scr[blk * D + k * 128: blk * D + (k + 1) * 128,
                                    bc:bc + 128])
                    mx_sems[slot].inc(i, 16)
                mx_tot[ti] = mx_sems[slot].v

            p6_prefetch(0)
            p6_prefetch(1)
            for fg in range(NFG):
                for ii in range(4):
                    ti = fg * 4 + ii
                    slot = ti % 2
                    tok0 = fg * 512 + ii * 128
                    mx_sems[slot].wait(nc.tensor, mx_tot[ti])
                    xh_sems[slot].wait(nc.vector, xh_tot[ti])
                    for k in range(NKT):
                        if k >= 2:
                            s_ve.wait(nc.tensor, v_mixadd[(ti, k - 2)])
                        elif ii > 0:
                            s_ve.wait(nc.tensor, v_x2m[ti - 1])
                        elif fg >= 1:
                            s_ve.wait(nc.tensor, x2cp[fg - 1])
                        i = nc.tensor.transpose(pt6s[k % 2][:, 0:128],
                                                mixts[slot][:, k * 128:(k + 1) * 128],
                                                ident[:])
                        pe_t = s_pe.inc(i)
                        s_pe.wait(nc.vector, pe_t)
                        i = nc.vector.tensor_add(s1[:, k * 128:(k + 1) * 128],
                                                 pt6s[k % 2][:, 0:128],
                                                 xhts[slot][:, k * 128:(k + 1) * 128])
                        v_mixadd[(ti, k)] = s_ve.inc(i)
                    if ti + 2 < NTI:
                        p6_prefetch(ti + 2)
                    s_ve.wait(nc.vector)
                    i = nc.vector.reduce_sum(lnc[:, 0:1], s1[:], axis=AX.X)
                    s_ve.inc(i)
                    s_ve.wait(nc.vector)
                    i = nc.vector.tensor_scalar_mul(lnc[:, 1:2], lnc[:, 0:1], 1.0 / D)
                    s_ve.inc(i)
                    s_ve.wait(nc.vector)
                    i = nc.vector.tensor_tensor(
                        out=s1[:], in0=s1[:],
                        in1=lnc[:, 1:2].broadcast_to([128, D]), op=OP.subtract)
                    s_ve.inc(i)
                    s_ve.wait(nc.scalar)
                    i = nc.scalar.activation(out=p6tmp[:, 0:D], in_=s1[:], func=AF.Square)
                    s_sc.inc(i)
                    s_sc.wait(nc.vector)
                    i = nc.vector.reduce_sum(lnc[:, 2:3], p6tmp[:, 0:D], axis=AX.X)
                    s_ve.inc(i)
                    s_ve.wait(nc.scalar)
                    i = nc.scalar.activation(out=lnc[:, 3:4], in_=lnc[:, 2:3],
                                             func=AF.Ln, scale=1.0 / D, bias=EPS)
                    s_sc.inc(i)
                    s_sc.wait(nc.scalar)
                    i = nc.scalar.activation(out=lnc[:, 4:5], in_=lnc[:, 3:4],
                                             func=AF.Exp, scale=-0.5)
                    s_sc.inc(i)
                    s_sc.wait(nc.vector)
                    if ii == 0 and fg >= 1:
                        s_pe.wait(nc.vector, pe_base_done[fg - 1])
                    i = nc.vector.tensor_tensor(
                        out=x2tm[:, ii * D:(ii + 1) * D], in0=s1[:],
                        in1=lnc[:, 4:5].broadcast_to([128, D]), op=OP.mult)
                    v_x2m[ti] = s_ve.inc(i)
                x2_done = s_ve.v
                s_ve.wait(nc.tensor, x2_done)
                v_x2c = {}
                for ii in range(4):
                    for k in range(NKT):
                        idx = ii * NKT + k
                        if idx >= 2:
                            s_ve.wait(nc.tensor, v_x2c[idx - 2])
                        else:
                            s_ve.wait(nc.tensor,
                                      v_mixadd[(fg * 4 + 3, NKT - 1)])
                        i = nc.tensor.transpose(
                            pt6s[k % 2][:, 0:128],
                            x2tm[:, ii * D + k * 128: ii * D + (k + 1) * 128],
                            ident[:])
                        pe_t = s_pe.inc(i)
                        s_pe.wait(nc.vector, pe_t)
                        i = nc.vector.tensor_copy(
                            x2f[:, k * 512 + ii * 128: k * 512 + (ii + 1) * 128],
                            pt6s[k % 2][:, 0:128])
                        v_x2c[idx] = s_ve.inc(i)
                x2cp[fg] = s_ve.v
                for k in range(NKT):
                    split_hilo(x2f[:, k * 512:(k + 1) * 512],
                               x2fh[:, k * 512:(k + 1) * 512],
                               x2fl[:, k * 512:(k + 1) * 512], p6tmp[:, 0:512])
                x2s_done = s_ve.v
                for ii in range(4):
                    ti = fg * 4 + ii
                    tok0 = fg * 512 + ii * 128
                    s_ve.wait(nc.tensor, x2s_done)
                    s_ve.wait(nc.tensor)
                    s_sc.wait(nc.tensor)
                    first = True
                    for (wa, xa) in ((rwh, x2fh), (rwh, x2fl), (rwl, x2fh)):
                        for k in range(NKT):
                            nc.tensor.matmul(pr8[:, 0:8],
                                             xa[:, k * 512 + ii * 128:
                                                k * 512 + (ii + 1) * 128],
                                             wa[:, k * 8:(k + 1) * 8],
                                             start=first,
                                             stop=(wa is rwl and k == NKT - 1))
                            first = False
                    first = True
                    ilast = None
                    for (wa, xa) in ((lah, x2fh), (lah, x2fl), (lal, x2fh)):
                        for k in range(NKT):
                            ilast = nc.tensor.matmul(pm8[:, 0:8],
                                                     xa[:, k * 512 + ii * 128:
                                                        k * 512 + (ii + 1) * 128],
                                                     wa[:, k * 8:(k + 1) * 8],
                                                     start=first,
                                                     stop=(wa is lal and k == NKT - 1))
                            first = False
                    pe_rm = s_pe.inc(ilast)
                    s_pe.wait(nc.vector, pe_rm)
                    i = nc.vector.tensor_add(gsm[:, 0:8], pr8[:, 0:8], routb[:])
                    s_ve.inc(i)
                    s_ve.wait(nc.vector)
                    i = nc.vector.reduce_max(gsm[:, 8:9], gsm[:, 0:8], axis=AX.X)
                    s_ve.inc(i)
                    s_ve.wait(nc.vector)
                    i = nc.vector.tensor_tensor(
                        out=gsm[:, 16:24], in0=gsm[:, 0:8],
                        in1=gsm[:, 8:9].broadcast_to([128, 8]), op=OP.subtract)
                    s_ve.inc(i)
                    s_ve.wait(nc.scalar)
                    i = nc.scalar.activation(out=gsm[:, 16:24], in_=gsm[:, 16:24],
                                             func=AF.Exp)
                    s_sc.inc(i)
                    s_sc.wait(nc.vector)
                    i = nc.vector.reduce_sum(gsm[:, 10:11], gsm[:, 16:24], axis=AX.X)
                    s_ve.inc(i)
                    s_ve.wait(nc.vector)
                    i = nc.vector.reciprocal(gsm[:, 11:12], gsm[:, 10:11])
                    s_ve.inc(i)
                    s_ve.wait(nc.vector)
                    i = nc.vector.tensor_tensor(
                        out=gsm[:, 24:32], in0=gsm[:, 16:24],
                        in1=gsm[:, 11:12].broadcast_to([128, 8]), op=OP.mult)
                    s_ve.inc(i)
                    s_ve.wait(nc.vector)
                    i = nc.vector.tensor_copy(gbf[:], gsm[:, 24:32])
                    s_ve.inc(i)
                    s_ve.wait(nc.vector)
                    i = nc.vector.tensor_copy(gsm[:, 32:40], gbf[:])
                    s_ve.inc(i)
                    s_pe.wait(nc.scalar, pe_rm)
                    i = nc.scalar.activation(out=gsm[:, 40:48], in_=pm8[:, 0:8],
                                             func=AF.Copy, scale=2.0)
                    sc_mid = s_sc.inc(i)
                    s_sc.wait(nc.vector, sc_mid)
                    s_ve.wait(nc.vector)
                    for e in range(8):
                        i = nc.vector.tensor_tensor(
                            out=wmid[:, e * 8:(e + 1) * 8], in0=gsm[:, 40:48],
                            in1=gsm[:, 32 + e:33 + e].broadcast_to([128, 8]),
                            op=OP.mult)
                    s_ve.inc(i)
                    split_hilo(wmid[:], wmh[:], wml[:], p6tmp[:, 0:64])
                    wm_done = s_ve.v
                    s_ve.wait(nc.tensor, wm_done)
                    i = nc.tensor.transpose(pw6[:, 0:128], wmh[:], ident_bf[:])
                    pe_w1 = s_pe.inc(i)
                    s_pe.wait(nc.scalar, pe_w1)
                    i = nc.scalar.copy(wmTh[:], pw6[:, 0:128])
                    sc_w1 = s_sc.inc(i)
                    s_sc.wait(nc.tensor, sc_w1)
                    i = nc.tensor.transpose(pw6[:, 0:128], wml[:], ident_bf[:])
                    pe_w2 = s_pe.inc(i)
                    s_pe.wait(nc.scalar, pe_w2)
                    i = nc.scalar.copy(wmTl[:], pw6[:, 0:128])
                    sc_w2 = s_sc.inc(i)
                    s_sc.wait(nc.tensor, sc_w2)
                    for oc in range(2):
                        ps = pfs[oc]
                        first = True
                        for (wa, xa) in ((bwh, x2fh), (bwh, x2fl), (bwl, x2fh)):
                            for k in range(NKT):
                                nc.tensor.matmul(
                                    ps[:, 0:512],
                                    xa[:, k * 512 + ii * 128: k * 512 + (ii + 1) * 128],
                                    wa[:, k * 1024 + oc * 512: k * 1024 + (oc + 1) * 512],
                                    start=first, stop=False)
                                first = False
                        ilast = None
                        for (wa, xa) in ((lbh, wmTh), (lbh, wmTl), (lbl, wmTh)):
                            ilast = nc.tensor.matmul(
                                ps[:, 0:512], xa[:], wa[:, oc * 512:(oc + 1) * 512],
                                start=False, stop=(wa is lbl))
                        pe_f = s_pe.inc(ilast)
                        pe_base_done[fg] = pe_f
                        s_pe.wait(nc.vector, pe_f)
                        i = nc.vector.tensor_add(ffn1[:, oc * 512:(oc + 1) * 512],
                                                 ps[:, 0:512],
                                                 basb[:, oc * 512:(oc + 1) * 512])
                        s_ve.inc(i)
                        s_ve.wait(nc.vector)
                        i = nc.vector.tensor_add(
                            ffn1[:, oc * 512:(oc + 1) * 512],
                            ffn1[:, oc * 512:(oc + 1) * 512],
                            x2tm[:, ii * D + oc * 512: ii * D + (oc + 1) * 512])
                        v_f = s_ve.inc(i)
                        s_ve.wait(nc.tensor, v_f)
                    s_ve.wait(nc.vector)
                    i = nc.vector.reduce_sum(lnc[:, 0:1], ffn1[:], axis=AX.X)
                    s_ve.inc(i)
                    s_ve.wait(nc.vector)
                    i = nc.vector.tensor_scalar_mul(lnc[:, 1:2], lnc[:, 0:1], 1.0 / D)
                    s_ve.inc(i)
                    s_ve.wait(nc.vector)
                    i = nc.vector.tensor_tensor(
                        out=ffn1[:], in0=ffn1[:],
                        in1=lnc[:, 1:2].broadcast_to([128, D]), op=OP.subtract)
                    s_ve.inc(i)
                    s_ve.wait(nc.scalar)
                    i = nc.scalar.activation(out=p6tmp[:, 0:D], in_=ffn1[:],
                                             func=AF.Square)
                    s_sc.inc(i)
                    s_sc.wait(nc.vector)
                    i = nc.vector.reduce_sum(lnc[:, 2:3], p6tmp[:, 0:D], axis=AX.X)
                    s_ve.inc(i)
                    s_ve.wait(nc.scalar)
                    i = nc.scalar.activation(out=lnc[:, 3:4], in_=lnc[:, 2:3],
                                             func=AF.Ln, scale=1.0 / D, bias=EPS)
                    s_sc.inc(i)
                    s_sc.wait(nc.scalar)
                    i = nc.scalar.activation(out=lnc[:, 4:5], in_=lnc[:, 3:4],
                                             func=AF.Exp, scale=-0.5)
                    s_sc.inc(i)
                    s_sc.wait(nc.vector)
                    if ti >= 2:
                        st_sems[ti % 2].wait(nc.vector, st_tot[ti - 2])
                    i = nc.vector.tensor_tensor(
                        out=outts[ti % 2][:], in0=ffn1[:],
                        in1=lnc[:, 4:5].broadcast_to([128, D]), op=OP.mult)
                    v_out = s_ve.inc(i)
                    s_ve.wait(nc.gpsimd, v_out)
                    i = nc.gpsimd.dma_start(out=out_half[tok0:tok0 + 128, :],
                                            in_=outts[ti % 2][:])
                    st_tot[ti] = st_sems[ti % 2].inc(i, 16)
            nc.gpsimd.wait_ge(s_st.h, s_st.v)
            nc.gpsimd.wait_ge(s_stb.h, s_stb.v)
            if taps:
                nc.gpsimd.wait_ge(s_sp.h, s_sp.v)
    return nc


_NC_CACHE = {}


def _get_nc(T=4096, taps=False):
    key = (T, taps)
    if key not in _NC_CACHE:
        _NC_CACHE[key] = build(T=T, taps=taps)
    return _NC_CACHE[key]


def make_in_maps(inp, T=4096):
    x = np.asarray(inp["L0_attn_x"], np.float32)
    w_in = np.asarray(inp["L0_attn_in_proj_w"], np.float32)
    conv_w = np.asarray(inp["L0_attn_conv_w"], np.float32)
    conv_b = np.asarray(inp["L0_attn_conv_b"], np.float32)
    xp_w = np.asarray(inp["L0_attn_x_proj_w"], np.float32)
    dt_w = np.asarray(inp["L0_attn_dt_proj_w"], np.float32)
    dt_b = np.asarray(inp["L0_attn_dt_proj_b"], np.float32)
    a_log = np.asarray(inp["L0_attn_A_log"], np.float32)
    d_v = np.asarray(inp["L0_attn_D"], np.float32)
    w_out = np.asarray(inp["L0_attn_out_proj_w"], np.float32)
    norm_w = np.asarray(inp["L0_attn_norm_w"], np.float32)
    base_w = np.asarray(inp["L0_ffn_base_w"], np.float32)
    base_b = np.asarray(inp["L0_ffn_base_b"], np.float32)
    lora_A = np.asarray(inp["L0_ffn_lora_A"], np.float32)
    lora_B = np.asarray(inp["L0_ffn_lora_B"], np.float32)
    router_w = np.asarray(inp["L0_ffn_router_w"], np.float32)
    router_b = np.asarray(inp["L0_ffn_router_b"], np.float32)
    TH = T // 2
    C = np.ascontiguousarray
    in_maps = []
    for c in range(8):
        b, r = c // 2, c % 2
        sl = slice(r * 1024, (r + 1) * 1024)
        in_maps.append({
            "xb": C(x[b, :T]),
            "xh": C(x[b, r * TH:(r + 1) * TH]),
            "w_in_u": C(w_in[:, sl]),
            "w_in_r": C(w_in[:, 2048 + r * 1024:2048 + (r + 1) * 1024]),
            "conv_w": C(conv_w[sl]),
            "conv_b": C(conv_b[sl].reshape(1024, 1)),
            "xp_w": C(xp_w[sl]),
            "dt_w": C(dt_w[:, sl]),
            "dt_b": C(dt_b[sl].reshape(1024, 1)),
            "a_log": C(a_log[sl]),
            "d_vec": C(d_v[sl].reshape(1024, 1)),
            "w_out": C(w_out[sl]),
            "base_wt": C(base_w.T),
            "base_b": C(base_b.reshape(1, 1024)),
            "lora_a": C(lora_A),
            "lora_b2": C(lora_B.reshape(64, 1024)),
            "rout_wt": C(router_w.T),
            "rout_b": C(router_b.reshape(1, 8)),
            "norm_w": C(norm_w.reshape(1024, 1)),
            "ident": np.eye(128, dtype=np.float32),
        })
    return in_maps


def kernel(**inp):
    T = 4096
    in_maps = make_in_maps(inp, T=T)
    nc = _get_nc(T=T, taps=False)
    res = run_bass_kernel_spmd(nc, in_maps, core_ids=list(range(8)))
    out = np.empty((4, 4096, 1024), np.float32)
    TH = T // 2
    for c in range(8):
        b, r = c // 2, c % 2
        out[b, r * TH:(r + 1) * TH] = res.results[c]["out_half"]
    return out
